# version 03 (final)
# baseline (speedup 1.0000x reference)
import math
import contextlib
import numpy as np
import ml_dtypes

import concourse.bass as bass
import concourse.mybir as mybir
from concourse.bass_utils import run_bass_kernel_spmd

F32 = mybir.dt.float32
BF16 = mybir.dt.bfloat16
I32 = mybir.dt.int32
AF = mybir.ActivationFunctionType
ALU = mybir.AluOpType
AX = mybir.AxisListType

D = 2048
TC = 2048
S = 8192
NCORES = 8
H = 16
DH = 128
KC = D // 128
NT = TC // 512
EI = 1408
EIC = EI // 128
NE = 8
IS = 5632
ISC = IS // 128
INV_SQRT_DH = 1.0 / math.sqrt(DH)
LN_EPS = 1e-5
GROUPS4 = [[0, 1, 2, 3], [4, 5, 6, 7]]
BF = ml_dtypes.bfloat16


class Prog:
    ENGINES = ("sy", "gp", "te", "sc", "ve")

    def __init__(self):
        self.q = {e: [] for e in self.ENGINES}
        self.c = {"cc": 0, "te": 0, "sc": 0, "ve": 0}
        self.d = {}

    def _flat(self, waits):
        flat = []
        for wx in waits:
            if wx is None:
                continue
            if isinstance(wx, list):
                flat.extend(w for w in wx if w is not None)
            else:
                flat.append(wx)
        out = []
        for s, v in flat:
            if v is None or v <= 0:
                continue
            cur = self.c.get(s, self.d.get(s))
            assert cur is not None and v <= cur, f"wait on future {s}={v} (now {cur})"
            out.append((s, v))
        return tuple(out)

    def emit(self, eng, fn, waits=(), inc=None, amt=1):
        waits = self._flat(waits)
        after = None
        if inc is not None:
            if inc in self.c:
                self.c[inc] += amt
                after = self.c[inc]
            else:
                self.d[inc] = self.d.get(inc, 0) + amt
                after = self.d[inc]
        self.q[eng].append((fn, waits, inc, amt))
        return after

    def dma(self, queue, sem, out, in_, waits=(), serial=True, **kw):
        w = list(waits)
        if serial and self.d.get(sem, 0) > 0:
            w.append((sem, self.d[sem]))
        return self.emit(queue, lambda h: h.dma_start(out=out, in_=in_, **kw),
                         waits=w, inc=sem, amt=16)

    def te(self, fn, waits=(), mark=True):
        return self.emit("te", fn, waits=waits, inc=("te" if mark else None))

    def sc(self, fn, waits=()):
        w = list(waits) + ([("sc", self.c["sc"])] if self.c["sc"] else [])
        return self.emit("sc", fn, waits=w, inc="sc")

    def ve(self, fn, waits=()):
        w = list(waits) + ([("ve", self.c["ve"])] if self.c["ve"] else [])
        return self.emit("ve", fn, waits=w, inc="ve")

    def snap(self):
        s = dict(self.c)
        s.update(self.d)
        return s

    def join(self, eng, snap=None):
        snap = snap or self.snap()
        w = [(s, v) for s, v in snap.items() if s != eng and v > 0]
        self.emit(eng, lambda h: None, waits=tuple(w))

    def barrier(self):
        snap = self.snap()
        for eng in self.ENGINES:
            self.join(eng, snap)
        return snap

    def lower(self, nc, sems, mksem):
        for name in list(self.d):
            if name not in sems:
                sems[name] = mksem(name)
        hw = {}

        def section(engname):
            def run(h):
                for fn, waits, inc, amt in self.q[engname]:
                    for s, v in waits:
                        if hw.get((engname, s), 0) < v:
                            h.wait_ge(sems[s], v)
                            hw[(engname, s)] = v
                    r = fn(h)
                    if inc is not None:
                        r.then_inc(sems[inc], amt)
            return run

        with nc.Block() as block:
            block.sync(section("sy"))
            block.gpsimd(section("gp"))
            block.tensor(section("te"))
            block.scalar(section("sc"))
            block.vector(section("ve"))


def build_nc(upto=4):
    nc = bass.Bass("TRN2", target_bir_lowering=False)

    def din(name, shape, dtype=F32):
        return nc.dram_tensor(name, shape, dtype, kind="ExternalInput").ap()

    def dint(name, shape, dtype):
        return nc.dram_tensor(name, shape, dtype, kind="Internal").ap()

    io = {}
    io["x0T"] = din("x0T", [D, TC])
    io["ropec"] = din("ropec", [128, TC], BF16)
    io["ropes"] = din("ropes", [128, TC], BF16)
    for nm in ["l0_wqT", "l0_wkT", "l0_wvT", "l0_woT"]:
        io[nm] = din(nm, [D, D])
    for nm in ["l0_bq", "l0_bk", "l0_bo"]:
        io[nm] = din(nm, [128, KC])
    io["l0_bv_row"] = din("l0_bv_row", [1, D])
    io["l0_gate_wT"] = din("l0_gate_wT", [D, NE])
    io["l0_w_gate"] = din("l0_w_gate", [NE, D, EI])
    io["l0_w_up"] = din("l0_w_up", [NE, D, EI])
    io["l0_w_down"] = din("l0_w_down", [NE, EI, D])
    io["l0_sw_gate"] = din("l0_sw_gate", [D, IS])
    io["l0_sw_up"] = din("l0_sw_up", [D, IS])
    io["l0_sw_down"] = din("l0_sw_down", [IS, D])
    io["l0_sgw"] = din("l0_sgw", [128, KC])
    for nm in ["l1_wqk", "l1_wv", "l1_wu", "l1_wo"]:
        io[nm] = din(nm, [D, D])
    for nm in ["l1_bqk", "l1_bu", "l1_bo", "l1_lng", "l1_lnb", "l1_sd"]:
        io[nm] = din(nm, [128, KC])
    io["l1_bv_row"] = din("l1_bv_row", [1, D])
    io["l1_wqgT"] = din("l1_wqgT", [D, IS], I32)
    io["l1_wquT"] = din("l1_wquT", [D, IS], I32)
    io["l1_wqdT"] = din("l1_wqdT", [IS, D], I32)
    io["l1_sg"] = din("l1_sg", [128, ISC])
    io["l1_su"] = din("l1_su", [128, ISC])
    io["ones_col_bf"] = din("ones_col_bf", [128, 1], BF16)
    io["ones_row_f"] = din("ones_row_f", [1, 128])
    io["ident_f"] = din("ident_f", [128, 128])

    yT = nc.dram_tensor("yT", [D, TC], F32, kind="ExternalOutput").ap()

    xfA = dint("xfA", [D, TC], F32)
    xfB = dint("xfB", [D, TC], F32)
    xb_s = dint("xb_s", [D, TC], BF16)
    q_s = dint("q_s", [D, TC], BF16)
    r_s = dint("r_s", [D, TC], F32)
    contrib = dint("contrib", [2 * D, TC], BF16)
    gathKV = dint("gathKV", [8 * D, TC], BF16)
    phi_f = dint("phi_f", [D, TC], BF16)
    phi_t = dint("phi_t", [TC, D], BF16)
    v_t = dint("v_t", [TC, D], BF16)
    u_f = dint("u_f", [D, TC], BF16)
    kv_c = dint("kv_c", [128, D], F32)
    kv_g = dint("kv_g", [4 * 128, D], F32)
    wg_bf = dint("wg_bf", [D, IS], BF16)
    wu_bf = dint("wu_bf", [D, IS], BF16)
    wd_bf = dint("wd_bf", [IS, D], BF16)

    P = Prog()

    with contextlib.ExitStack() as ctx:
        e = ctx.enter_context
        A_x = e(nc.sbuf_tensor("A_x", [128, 32768], BF16))
        A_w = e(nc.sbuf_tensor("A_w", [128, 16384], BF16))
        A_y = e(nc.sbuf_tensor("A_y", [128, 32768], BF16))
        A_f = e(nc.sbuf_tensor("A_f", [128, 6336], F32))
        A_t = e(nc.sbuf_tensor("A_t", [128, 4096], BF16))
        A_st = e(nc.sbuf_tensor("A_st", [128, 1024], F32))
        A_bias = e(nc.sbuf_tensor("A_bias", [128, 320], F32))
        ones_cb = e(nc.sbuf_tensor("ones_cb", [128, 1], BF16))
        ones_rf = e(nc.sbuf_tensor("ones_rf", [1, 128], F32))
        ident = e(nc.sbuf_tensor("ident", [128, 128], F32))
        rope_c = e(nc.sbuf_tensor("rope_c", [128, TC], BF16))
        rope_s = e(nc.sbuf_tensor("rope_s", [128, TC], BF16))

        ps_a = e(nc.psum_tensor("ps_a", [128, 512], F32))
        ps_b = e(nc.psum_tensor("ps_b", [128, 512], F32))
        ps_w = e(nc.psum_tensor("ps_w", [128, 2048], F32))
        ps_v = e(nc.psum_tensor("ps_v", [128, 1024], F32))

        sems = {}
        for s_ in ("cc", "te", "sc", "ve"):
            sems[s_] = e(nc.semaphore(s_))

        def mksem(name):
            return e(nc.semaphore(name))

        BOFF = {"l0_bq": 0, "l0_bk": 16, "l0_bo": 32, "l1_bqk": 48, "l1_bu": 64,
                "l1_bo": 80, "l1_lng": 96, "l1_lnb": 112, "l1_sd": 128,
                "negC": 144, "l0_sgw": 160, "l1_sg": 176, "l1_su": 220}
        ZB, ZR, ZC, ZM = 0, 2048, 6144, 6272

        def bias_ap(name, col):
            return A_bias[:, BOFF[name] + col: BOFF[name] + col + 1]

        def zb(slot, w=512):
            return A_f[:, ZB + slot * 512: ZB + slot * 512 + w]

        def row(off, n):
            return A_f[0:1, ZR + off: ZR + off + n]

        def mrow(i):
            return A_f[0:1, ZM + i * 4: ZM + i * 4 + 1]

        def at(slot, w=512):
            return A_t[:, slot * 512: slot * 512 + w]

        def st(slot):
            return A_st[:, slot * 512: slot * 512 + 512]

        for nm, dst in [("ones_col_bf", ones_cb), ("ones_row_f", ones_rf),
                        ("ident_f", ident), ("ropec", rope_c), ("ropes", rope_s)]:
            P.dma("sy", "init", dst[:], io[nm], serial=False)
        for nm, ncols in [("l0_bq", KC), ("l0_bk", KC), ("l0_bo", KC),
                          ("l1_bqk", KC), ("l1_bu", KC), ("l1_bo", KC),
                          ("l1_lng", KC), ("l1_lnb", KC), ("l1_sd", KC),
                          ("l0_sgw", KC), ("l1_sg", ISC), ("l1_su", ISC)]:
            P.dma("sy", "init", A_bias[:, BOFF[nm]:BOFF[nm] + ncols], io[nm],
                  serial=False)
        c_init = P.d["init"]
        for i in range(4):
            P.dma("gp", "xc", xb_s[i * 512:(i + 1) * 512, :],
                  io["x0T"][i * 512:(i + 1) * 512, :], serial=False)
        c_xb0 = P.d["xc"]
        for eng in ("te", "sc", "ve"):
            P.emit(eng, lambda h: None, waits=[("init", c_init)])

        def load_wslice(wdram, m0, mw, slot_cols, nkc, sem, waits=()):
            src = wdram[:, m0:m0 + mw].rearrange("(k p) m -> p k m", p=128)
            dst = A_w[:, slot_cols:slot_cols + nkc * mw].rearrange(
                "p (k m) -> p k m", k=nkc)
            return P.dma("gp", sem, dst, src, waits=waits)

        def load_xb_full(wait):
            src = xb_s.rearrange("(k p) t -> p k t", p=128)
            dst = A_x[:].rearrange("p (k t) -> p k t", k=KC)
            return P.dma("sy", "xbld", dst, src, waits=wait)

        def xb_tile(k, nt, w=512):
            return A_x[:, k * TC + nt * 512: k * TC + nt * 512 + w]

        def proj_like(w_dram, rhs_tile_fn, post_fn, nkc=KC, n_obl=4,
                      ocs_per_blk=4, nts=NT, w_waits=(), rhs_ready=(),
                      ps_init=(None, None)):
            slot = [0, 8192]
            wsem = ["w0", "w1"]
            mw = ocs_per_blk * 128
            wload = [None, None]
            wload[0] = load_wslice(w_dram, 0, mw, slot[0], nkc, "w0", waits=w_waits)
            if n_obl > 1:
                wload[1] = load_wslice(w_dram, mw, mw, slot[1], nkc, "w1",
                                       waits=w_waits)
            last_cons = {0: ps_init[0], 1: ps_init[1]}
            G = 0
            for ob in range(n_obl):
                blk_last_te = None
                for oc_i in range(ocs_per_blk):
                    oc = ob * ocs_per_blk + oc_i
                    for nt in range(nts):
                        par = G % 2
                        ps = ps_a if par == 0 else ps_b
                        w0 = [(wsem[ob % 2], wload[ob % 2])] + list(rhs_ready)
                        if last_cons[par] is not None:
                            w0.append(last_cons[par])
                        tec = None
                        for k in range(nkc):
                            c0 = slot[ob % 2] + k * mw + oc_i * 128
                            lhsT = A_w[:, c0:c0 + 128]
                            tec = P.te(
                                lambda h, lhsT=lhsT, ps=ps, rt=rhs_tile_fn(k, nt),
                                st_=(k == 0), sp=(k == nkc - 1): h.matmul(
                                    ps[:], lhsT, rt, start=st_, stop=sp),
                                waits=(w0 if k == 0 else ()),
                                mark=(k == nkc - 1))
                        blk_last_te = tec
                        last_cons[par] = post_fn(oc, nt, ps, tec)
                        G += 1
                if ob + 2 < n_obl:
                    wload[ob % 2] = load_wslice(
                        w_dram, (ob + 2) * mw, mw, slot[ob % 2], nkc,
                        wsem[ob % 2], waits=[("te", blk_last_te)] + list(w_waits))
            return last_cons

        def bcast_row_f32(src_row, n, ps_cols=0, waits=()):
            last = None
            for i in range(0, n, 512):
                w = min(512, n - i)
                last = P.te(lambda h, i=i, w=w: h.matmul(
                    ps_w[:, ps_cols + i: ps_cols + i + w], ones_rf[:],
                    src_row[0:1, i:i + w], start=True, stop=True), waits=waits)
                waits = ()
            return last

        class PostLN:

            def __init__(self, tok0=0, ntoks=TC, init_guard=None):
                self.tok0 = tok0
                self.nts = ntoks // 512
                self.G = 0
                self.row_pair_cons = [None, None]
                self.rbf_cons = [init_guard, init_guard]
                self.st_cons = [init_guard, init_guard]
                self.first = [True] * self.nts
                self.a_done = None

            def st_slot_wait(self):
                return self.st_cons[self.G % 2]

            def chunk(self, oc, nt, r_marker):
                g2 = self.G % 2
                rbf = at(g2)
                m_rbf = P.sc(lambda h, rbf=rbf, sl=st(g2): h.copy(rbf, sl),
                             waits=[r_marker, self.rbf_cons[g2]])
                rsq = at(2)
                m_rsq = P.ve(lambda h, rsq=rsq, rbf=rbf: h.tensor_mul(rsq, rbf, rbf),
                             waits=[("sc", m_rbf), self.rsq_guard()])
                ssum = ps_w[0:1, g2 * 1024: g2 * 1024 + 512]
                ssq = ps_w[0:1, g2 * 1024 + 512: g2 * 1024 + 1024]
                m_t1 = P.te(lambda h, ssum=ssum, rbf=rbf: h.matmul(
                    ssum, ones_cb[:], rbf, start=True, stop=True),
                    waits=[("sc", m_rbf), self.row_pair_cons[g2]])
                m_t2 = P.te(lambda h, ssq=ssq, rsq=rsq: h.matmul(
                    ssq, ones_cb[:], rsq, start=True, stop=True),
                    waits=[("ve", m_rsq)])
                self._last_t2 = m_t2
                sr = row(nt * 512, 512)
                qr = row(2048 + nt * 512, 512)
                if self.first[nt]:
                    m_a1 = P.ve(lambda h, sr=sr, ssum=ssum: h.tensor_copy(sr, ssum),
                                waits=[("te", m_t1)])
                    m_a2 = P.ve(lambda h, qr=qr, ssq=ssq: h.tensor_copy(qr, ssq),
                                waits=[("te", m_t2)])
                    self.first[nt] = False
                else:
                    m_a1 = P.ve(lambda h, sr=sr, ssum=ssum: h.tensor_add(sr, ssum, sr),
                                waits=[("te", m_t1)])
                    m_a2 = P.ve(lambda h, qr=qr, ssq=ssq: h.tensor_add(qr, ssq, qr),
                                waits=[("te", m_t2)])
                self.row_pair_cons[g2] = ("ve", m_a2)
                self.rbf_cons[g2] = [("ve", m_rsq), ("te", m_t1)]
                dst = r_s[oc * 128:(oc + 1) * 128,
                          self.tok0 + nt * 512: self.tok0 + nt * 512 + 512]
                m_d = P.dma("sy", "rs%d" % g2, dst, st(g2),
                            waits=[("ve", m_rsq)])
                self.st_cons[g2] = [("rs%d" % g2, m_d), ("sc", m_rbf)]
                self.G += 1
                self.a_done = ("ve", m_a2)
                return ("sc", m_rbf)

            def rsq_guard(self):
                t2 = getattr(self, "_last_t2", None)
                return ("te", t2) if t2 is not None else None

            def passB(self, dst_f32, dst_bf):
                stats_done = self.a_done
                sc_snap = P.c["sc"]
                last = None
                rd_cons = [None, None]
                for nt in range(self.nts):
                    sr = row(nt * 512, 512)
                    qr = row(2048 + nt * 512, 512)
                    tr = A_f[0:1, ZB + 1536: ZB + 2048]
                    m1 = P.ve(lambda h, sr=sr: h.tensor_scalar(
                        sr, sr, 1.0 / D, None, ALU.mult), waits=[stats_done])
                    m2 = P.ve(lambda h, qr=qr: h.tensor_scalar(
                        qr, qr, 1.0 / D, None, ALU.mult))
                    m3 = P.ve(lambda h, tr=tr, sr=sr: h.tensor_mul(tr, sr, sr))
                    m4 = P.ve(lambda h, qr=qr, tr=tr: h.scalar_tensor_tensor(
                        qr, qr, LN_EPS, tr, ALU.add, ALU.subtract))
                    m5 = P.sc(lambda h, tr=tr, qr=qr: h.activation(
                        tr, qr, AF.Sqrt, bias=0.0, scale=1.0),
                        waits=[("ve", m4)])
                    m6 = P.ve(lambda h, qr=qr, tr=tr: h.reciprocal(qr, tr),
                              waits=[("sc", m5)])
                    b1 = bcast_row_f32(sr, 512, 0, waits=[("ve", m6)])
                    b2 = bcast_row_f32(qr, 512, 512)
                    mub, rsb = zb(0), zb(1)
                    e1 = P.ve(lambda h, mub=mub: h.tensor_copy(mub, ps_w[:, 0:512]),
                              waits=[("te", b1)])
                    e2 = P.ve(lambda h, rsb=rsb: h.tensor_copy(rsb, ps_w[:, 512:1024]),
                              waits=[("te", b2)])
                    for oc in range(KC):
                        g2 = oc % 2
                        src = r_s[oc * 128:(oc + 1) * 128,
                                  self.tok0 + nt * 512: self.tok0 + nt * 512 + 512]
                        wl = [rd_cons[g2]]
                        if nt == 0 and oc < 2:
                            wl += [self.st_cons[0], self.st_cons[1],
                                   ("sc", sc_snap)]
                        m_l = P.dma("sy", "rl%d" % g2, st(g2), src, waits=wl)
                        t3 = zb(2)
                        m_s = P.ve(lambda h, t3=t3, sl=st(g2), mub=mub: h.tensor_sub(
                            t3, sl, mub), waits=[("rl%d" % g2, m_l), ("ve", e2)])
                        m_y = P.ve(lambda h, sl=st(g2), t3=t3, rsb=rsb: h.tensor_mul(
                            sl, t3, rsb))
                        dst = dst_f32[oc * 128:(oc + 1) * 128,
                                      self.tok0 + nt * 512: self.tok0 + nt * 512 + 512]
                        m_o = P.dma("sy", "yo%d" % g2, dst, st(g2),
                                    waits=[("ve", m_y)])
                        rd = [("yo%d" % g2, m_o)]
                        if dst_bf is not None:
                            yb = at(3)
                            m_c = P.sc(lambda h, yb=yb, sl=st(g2): h.copy(yb, sl),
                                       waits=[("ve", m_y),
                                              getattr(self, "_yb_guard", None)])
                            dstb = dst_bf[oc * 128:(oc + 1) * 128,
                                          self.tok0 + nt * 512: self.tok0 + nt * 512 + 512]
                            m_ob = P.dma("sy", "ybo", dstb, yb,
                                         waits=[("sc", m_c)])
                            self._yb_guard = ("ybo", m_ob)
                            rd.append(("ybo", m_ob))
                        rd_cons[g2] = rd
                        last = rd
                return last

        def emit_l0_attn(x_src, dst_f32, dst_bf):
            c_xb = P.dma("gp", "xbf",
                         A_x[:].rearrange("p (k t) -> p k t", k=KC),
                         io["x0T"].rearrange("(k p) t -> p k t", p=128))
            rhs_ready = [("xbf", c_xb)]

            def qk_post(bias_name, dst_dram, osem, init_guard=None):
                state = {"A": [init_guard, init_guard], "O": init_guard}

                def post(oc, nt, ps, tec):
                    par = (oc * NT + nt) % 2
                    qt = at(par)
                    sw = at(2)
                    t1 = A_t[:, 2560:3072]
                    t2 = A_t[:, 3072:3584]
                    out = A_t[:, 3584:4096]
                    m1 = P.sc(lambda h, qt=qt, ps=ps, oc=oc: h.activation(
                        qt, ps[:], AF.Identity, bias=bias_ap(bias_name, oc),
                        scale=1.0), waits=[("te", tec), state["A"][par]])
                    P.ve(lambda h, sw=sw, qt=qt: h.tensor_copy(
                        sw[0:64, :], qt[64:128, :]), waits=[("sc", m1)])
                    P.ve(lambda h, sw=sw, qt=qt: h.tensor_copy(
                        sw[64:128, :], qt[0:64, :]))
                    mt1 = P.ve(lambda h, t1=t1, qt=qt, nt=nt: h.tensor_mul(
                        t1, qt, rope_c[:, nt * 512:(nt + 1) * 512]))
                    P.ve(lambda h, t2=t2, sw=sw, nt=nt: h.tensor_mul(
                        t2, sw, rope_s[:, nt * 512:(nt + 1) * 512]))
                    ma = P.ve(lambda h, out=out, t1=t1, t2=t2: h.tensor_add(
                        out, t1, t2), waits=[state["O"]])
                    dst = dst_dram[oc * 128:(oc + 1) * 128, nt * 512:(nt + 1) * 512]
                    md = P.dma("sy", osem, dst, out, waits=[("ve", ma)])
                    state["A"][par] = ("ve", mt1)
                    state["O"] = (osem, md)
                    return ("sc", m1)
                return post

            q_cons = proj_like(io["l0_wqT"], xb_tile,
                               qk_post("l0_bq", q_s, "qso"), rhs_ready=rhs_ready)
            te_q = P.c["te"]
            k_cons = proj_like(io["l0_wkT"], xb_tile,
                               qk_post("l0_bk", contrib, "kso",
                                       init_guard=[("ve", P.c["ve"]),
                                                   ("qso", P.d.get("qso", 0))]),
                               rhs_ready=rhs_ready, w_waits=[("te", te_q)],
                               ps_init=(q_cons[0], q_cons[1]))
            te_k = P.c["te"]

            bvrow = row(0, D)
            c_bv = P.dma("sy", "misc", bvrow, io["l0_bv_row"])
            tb = bcast_row_f32(bvrow, D, 0, waits=[("misc", c_bv)])
            m_bvb = P.ve(lambda h: h.tensor_copy(A_f[:, ZB:ZB + 2048],
                                                 ps_w[:, 0:2048]),
                         waits=[("te", tb)])
            state_v = {"cons": [k_cons[0], k_cons[1]], "slot": [None, None]}
            slotv = [0, 8192]
            wloadv = [load_wslice(io["l0_wvT"], 0, 512, slotv[0], KC, "w0",
                                  waits=[("te", te_k)]),
                      load_wslice(io["l0_wvT"], 512, 512, slotv[1], KC, "w1",
                                  waits=[("te", te_k)])]
            Gv = 0
            for ovb in range(4):
                blk_te = None
                for tc in range(KC):
                    par = Gv % 2
                    ps = ps_a if par == 0 else ps_b
                    wts = [(("w0" if ovb % 2 == 0 else "w1"), wloadv[ovb % 2]),
                           ("xbf", c_xb), state_v["cons"][par]]
                    tec = None
                    for k in range(KC):
                        lhsT = A_x[:, k * TC + tc * 128: k * TC + tc * 128 + 128]
                        rt = A_w[:, slotv[ovb % 2] + k * 512:
                                 slotv[ovb % 2] + (k + 1) * 512]
                        tec = P.te(lambda h, ps=ps, lhsT=lhsT, rt=rt,
                                   st_=(k == 0), sp=(k == KC - 1): h.matmul(
                                       ps[:], lhsT, rt, start=st_, stop=sp),
                                   waits=(wts if k == 0 else ()),
                                   mark=(k == KC - 1))
                    blk_te = tec
                    vout = at(par)
                    mv = P.ve(lambda h, vout=vout, ps=ps, ovb=ovb: h.tensor_add(
                        vout, ps[:], A_f[:, ZB + ovb * 512: ZB + (ovb + 1) * 512]),
                        waits=[("te", tec), ("ve", m_bvb), state_v["slot"][par]])
                    dst = contrib[D + tc * 128: D + (tc + 1) * 128,
                                  ovb * 512:(ovb + 1) * 512]
                    md = P.dma("sy", "vso%d" % par, dst, vout, waits=[("ve", mv)])
                    state_v["cons"][par] = ("ve", mv)
                    state_v["slot"][par] = ("vso%d" % par, md)
                    Gv += 1
                if ovb + 2 < 4:
                    wloadv[ovb % 2] = load_wslice(
                        io["l0_wvT"], (ovb + 2) * 512, 512, slotv[ovb % 2], KC,
                        "w0" if ovb % 2 == 0 else "w1", waits=[("te", blk_te)])

            snapG = P.snap()
            P.emit("gp", lambda h: None,
                   waits=[("kso", P.d.get("kso", 0)),
                          ("vso0", P.d.get("vso0", 0)),
                          ("vso1", P.d.get("vso1", 0))])
            c_cc = P.emit("gp", lambda h: h.collective_compute(
                "AllGather", ALU.bypass, replica_groups=GROUPS4,
                ins=[contrib.opt()], outs=[gathKV.opt()]), inc="cc")

            KW = [0, 2048]
            VW = [4096, 6144]
            QS = 0
            PT = [2048, 3072]
            att_cons = None
            for h_ in range(H):
                mq, mk, c2, tmpc = mrow(0), mrow(1), mrow(2), mrow(3)
                cq_w = [att_cons] if att_cons else [
                    ("ve", snapG["ve"]), ("sc", snapG["sc"]),
                    ("vso0", snapG.get("vso0", 0)), ("vso1", snapG.get("vso1", 0))]
                cq_w = cq_w + [("qso", P.d.get("qso", 0))]
                c_q = P.dma("sy", "fq", A_t[:, QS:QS + 2048],
                            q_s[h_ * 128:(h_ + 1) * 128, :], waits=cq_w)
                ksq = A_y[:, h_ * 2048:(h_ + 1) * 2048]
                win_cons = [None, None]
                m_m = None
                prev_red = None
                for w in range(4):
                    par = w % 2
                    c_k = P.dma("sy", "fk%d" % par,
                                A_w[:, KW[par]:KW[par] + 2048],
                                gathKV[w * 4096 + h_ * 128:
                                       w * 4096 + (h_ + 1) * 128, :],
                                waits=[("cc", c_cc), win_cons[par],
                                       (att_cons if w < 2 else None)])
                    kbuf = A_w[:, KW[par]:KW[par] + 2048]
                    m_sq = P.ve(lambda h, ksq=ksq, kbuf=kbuf: h.tensor_mul(
                        ksq, kbuf, kbuf), waits=[("fk%d" % par, c_k), att_cons])
                    win_cons[par] = ("ve", m_sq)
                    for g in range(4):
                        pr = ps_a[0:1, 0:512]
                        m_n = P.te(lambda h, pr=pr, ksq=ksq, g=g: h.matmul(
                            pr, ones_cb[:], ksq[:, g * 512:(g + 1) * 512],
                            start=True, stop=True),
                            waits=[("ve", m_sq), prev_red])
                        if m_m is None:
                            m_m = P.ve(lambda h, mk=mk, pr=pr: h.reduce_max(
                                mk, pr, axis=AX.X), waits=[("te", m_n)])
                        else:
                            P.ve(lambda h, tmpc=tmpc, pr=pr: h.reduce_max(
                                tmpc, pr, axis=AX.X), waits=[("te", m_n)])
                            m_m = P.ve(lambda h, mk=mk, tmpc=tmpc: h.tensor_max(
                                mk, mk, tmpc))
                        prev_red = ("ve", m_m)
                m_qs = P.ve(lambda h, ksq=ksq: h.tensor_mul(
                    ksq, A_t[:, QS:QS + 2048], A_t[:, QS:QS + 2048]),
                    waits=[("fq", c_q), m_m and ("ve", m_m)])
                m_q = None
                for g in range(4):
                    pr = ps_a[0:1, 0:512]
                    m_n = P.te(lambda h, pr=pr, ksq=ksq, g=g: h.matmul(
                        pr, ones_cb[:], ksq[:, g * 512:(g + 1) * 512],
                        start=True, stop=True),
                        waits=[("ve", m_qs), prev_red])
                    if m_q is None:
                        m_q = P.ve(lambda h, mq=mq, pr=pr: h.reduce_max(
                            mq, pr, axis=AX.X), waits=[("te", m_n)])
                    else:
                        P.ve(lambda h, tmpc=tmpc, pr=pr: h.reduce_max(
                            tmpc, pr, axis=AX.X), waits=[("te", m_n)])
                        m_q = P.ve(lambda h, mq=mq, tmpc=tmpc: h.tensor_max(
                            mq, mq, tmpc))
                    prev_red = ("ve", m_q)
                m_c2 = P.ve(lambda h, c2=c2, mq=mq, mk=mk: h.tensor_mul(c2, mq, mk))
                m_c = P.sc(lambda h, c2=c2: h.activation(
                    c2, c2, AF.Sqrt, bias=0.0, scale=1.0), waits=[("ve", m_c2)])
                m_cn = P.sc(lambda h, c2=c2: h.mul(c2, c2, -INV_SQRT_DH))
                m_bc = P.te(lambda h, c2=c2: h.matmul(
                    ps_a[:, 0:1], ones_rf[:], c2, start=True, stop=True),
                    waits=[("sc", m_cn)])
                m_nc = P.ve(lambda h, h_=h_: h.tensor_copy(
                    bias_ap("negC", h_), ps_a[:, 0:1]), waits=[("te", m_bc)])
                negC = bias_ap("negC", h_)

                kuse = list(win_cons)
                vuse = [None, None]
                kload = [None, None]
                vload = [None, None]
                for qg in range(2):
                    exp_done = {}
                    sums_done = {}
                    loaded = [-1]

                    def ensure(w):
                        if w > 3 or w <= loaded[0]:
                            return
                        loaded[0] = w
                        wp = w % 2
                        kload[wp] = P.dma(
                            "sy", "fk%d" % wp,
                            A_w[:, KW[wp]:KW[wp] + 2048],
                            gathKV[w * 4096 + h_ * 128:
                                   w * 4096 + (h_ + 1) * 128, :],
                            waits=[("ve", m_nc), kuse[wp]])
                        srcv = gathKV[w * 4096 + D: w * 4096 + 2 * D,
                                      h_ * 128:(h_ + 1) * 128].rearrange(
                            "(c p) d -> p c d", p=128)
                        dstv = A_w[:, VW[wp]:VW[wp] + 2048].rearrange(
                            "p (c d) -> p c d", c=16)
                        vload[wp] = P.dma("sy", "fv%d" % wp, dstv, srcv,
                                          waits=[("cc", c_cc), vuse[wp]])

                    def emit_av(kt, pts, wpar, m_e):
                        lhsV = A_w[:, VW[wpar] + (kt % 16) * 128:
                                   VW[wpar] + (kt % 16 + 1) * 128]
                        P.te(lambda h, lhsV=lhsV, pts=pts, kt=kt: h.matmul(
                            ps_v[:, 0:512], lhsV, pts[:, 0:512],
                            start=(kt == 0), stop=(kt == 63)),
                            waits=[("sc", m_e), ("fv%d" % wpar, vload[wpar])],
                            mark=False)
                        P.te(lambda h, lhsV=lhsV, pts=pts, kt=kt: h.matmul(
                            ps_v[:, 512:1024], lhsV, pts[:, 512:1024],
                            start=(kt == 0), stop=(kt == 63)), mark=False)
                        P.te(lambda h, pts=pts, kt=kt: h.matmul(
                            ps_a[0:1, 0:512], ones_cb[:], pts[:, 0:512],
                            start=(kt == 0), stop=(kt == 63)), mark=False)
                        s2 = P.te(lambda h, pts=pts, kt=kt: h.matmul(
                            ps_b[0:1, 0:512], ones_cb[:], pts[:, 512:1024],
                            start=(kt == 0), stop=(kt == 63)))
                        sums_done[kt] = s2
                        vuse[wpar] = ("te", s2)

                    pend = None
                    for kt in range(64):
                        w = kt // 16
                        it = kt % 16
                        wpar = w % 2
                        if it == 0:
                            if pend is not None:
                                emit_av(*pend)
                                pend = None
                            ensure(w)
                            ensure(w + 1)
                        spar = kt % 2
                        sps = ps_w[:, spar * 1024:(spar + 1) * 1024]
                        wsc = [("fk%d" % wpar, kload[wpar]), ("fq", c_q),
                               (("sc", exp_done[kt - 2]) if kt - 2 in exp_done
                                else None)]
                        lhsK = A_w[:, KW[wpar] + it * 128: KW[wpar] + (it + 1) * 128]
                        P.te(lambda h, sps=sps, lhsK=lhsK, qg=qg: h.matmul(
                            sps[:, 0:512], lhsK,
                            A_t[:, QS + qg * 1024: QS + qg * 1024 + 512],
                            start=True, stop=True), waits=wsc, mark=False)
                        t2 = P.te(lambda h, sps=sps, lhsK=lhsK, qg=qg: h.matmul(
                            sps[:, 512:1024], lhsK,
                            A_t[:, QS + qg * 1024 + 512: QS + qg * 1024 + 1024],
                            start=True, stop=True))
                        pts = A_t[:, PT[kt % 2]:PT[kt % 2] + 1024]
                        m_e = P.sc(lambda h, pts=pts, sps=sps, negC=negC: h.activation(
                            pts, sps[:], AF.Exp, bias=negC, scale=INV_SQRT_DH),
                            waits=[("te", t2),
                                   (("te", sums_done[kt - 2]) if kt - 2 in sums_done
                                    else None)])
                        exp_done[kt] = m_e
                        kuse[wpar] = ("te", t2)
                        if pend is not None:
                            emit_av(*pend)
                        pend = (kt, pts, wpar, m_e)
                    emit_av(*pend)
                    pend = None
                    rr = row(3584, 1024) if False else A_f[0:1, ZR + 1024:ZR + 2048]
                    m_r1 = P.ve(lambda h, rr=rr: h.reciprocal(
                        rr[0:1, 0:512], ps_a[0:1, 0:512]),
                        waits=[("te", sums_done[63])])
                    m_r2 = P.ve(lambda h, rr=rr: h.reciprocal(
                        rr[0:1, 512:1024], ps_b[0:1, 0:512]))
                    bb = bcast_row_f32(rr, 1024, 0, waits=[("ve", m_r2)])
                    m_rb = P.ve(lambda h: h.tensor_copy(
                        A_f[:, ZB:ZB + 1024], ps_w[:, 0:1024]), waits=[("te", bb)])
                    attdst = A_y[:, h_ * 2048 + qg * 1024:
                                 h_ * 2048 + qg * 1024 + 1024]
                    m_att = P.ve(lambda h, attdst=attdst: h.tensor_mul(
                        attdst, ps_v[:], A_f[:, ZB:ZB + 1024]), waits=[("ve", m_rb)])
                    att_cons = ("ve", m_att)
                    P.emit("te", lambda h: None, waits=[("ve", m_att)])

            pl = PostLN(init_guard=att_cons)
            pl._yb_guard = ("xc", c_xb0)
            xpf = {"load": {}, "use": {}}

            def xload(G):
                if G in xpf["load"] or G >= 64:
                    return
                s4 = G % 4
                oc2, nt2 = G // NT, G % NT
                xpf["load"][G] = P.dma(
                    "sy", "ox%d" % s4, zb(s4),
                    x_src[oc2 * 128:(oc2 + 1) * 128,
                          nt2 * 512:(nt2 + 1) * 512],
                    waits=[xpf["use"].get(G - 4, att_cons)])

            def oproj_post(oc, nt, ps, tec):
                G = pl.G
                for gg in (G, G + 1, G + 2):
                    xload(gg)
                s4 = G % 4
                g2 = G % 2
                m2 = P.ve(lambda h, ps=ps, s4=s4, g2=g2, oc=oc:
                          h.scalar_tensor_tensor(
                              st(g2), ps[:], bias_ap("l0_bo", oc), zb(s4),
                              ALU.add, ALU.add),
                          waits=[("te", tec), ("ox%d" % s4, xpf["load"][G]),
                                 pl.st_slot_wait()])
                xpf["use"][G] = ("ve", m2)
                return pl.chunk(oc, nt, ("ve", m2))

            def att_rhs(k, nt):
                return A_y[:, k * 2048 + nt * 512: k * 2048 + nt * 512 + 512]

            proj_like(io["l0_woT"], att_rhs, oproj_post,
                      rhs_ready=[att_cons], w_waits=[("te", P.c["te"])],
                      ps_init=(att_cons, att_cons))
            pl.passB(dst_f32, dst_bf)

        def emit_l0_ffn(x_src, dst_f32, dst_bf):
            P.barrier()
            gw = A_f[:, 2048:2176]
            c_gw = P.dma("sy", "misc",
                         gw.rearrange("p (k e) -> p k e", k=KC),
                         io["l0_gate_wT"].rearrange("(k p) e -> p k e", p=128))
            lg = A_f[:, 2176:2184]
            el = A_f[:, 2184:2192]
            sel = A_f[:, 2192:2200]
            gt8 = A_f[:, 2200:2208]
            eqb = A_f[:, 2208:2216]
            cw = A_f[:, 2216:2224]
            cnt1 = A_f[:, 2224:2225]
            cnt2 = A_f[:, 2225:2226]
            nmx = A_f[:, 2226:2227]
            xblk = A_f[:, ZB:ZB + 2048]
            prev_te = None
            prev_ps = None
            for tc in range(KC):
                c_x = P.dma("sy", "misc",
                            xblk.rearrange("p (k t) -> p k t", k=KC),
                            x_src[:, tc * 128:(tc + 1) * 128].rearrange(
                                "(k p) t -> p k t", p=128),
                            waits=[prev_te, ("misc", c_gw)])
                tec = None
                for k in range(KC):
                    tec = P.te(lambda h, k=k: h.matmul(
                        ps_a[:, 0:8], A_f[:, ZB + k * 128:ZB + (k + 1) * 128],
                        gw[:, k * 8:(k + 1) * 8],
                        start=(k == 0), stop=(k == KC - 1)),
                        waits=([("misc", c_x), prev_ps] if k == 0 else ()),
                        mark=(k == KC - 1))
                prev_te = ("te", tec)
                m_lg = P.ve(lambda h: h.tensor_copy(lg, ps_a[:, 0:8]),
                            waits=[("te", tec)])
                prev_ps = ("ve", m_lg)
                P.ve(lambda h: h.reduce_max(nmx, lg, axis=AX.X, negate=True))
                m_el = P.sc(lambda h: h.activation(el, lg, AF.Exp, bias=nmx,
                                                   scale=1.0),
                            waits=[("ve", P.c["ve"])])
                m_last = None
                for ei in range(NE):
                    P.ve(lambda h, ei=ei: h.tensor_single_scalar(
                        gt8, lg, lg[:, ei:ei + 1], ALU.is_gt),
                        waits=[("sc", m_el)] if ei == 0 else [])
                    P.ve(lambda h: h.reduce_sum(cnt1, gt8, axis=AX.X))
                    if ei > 0:
                        P.ve(lambda h, ei=ei: h.tensor_single_scalar(
                            eqb[:, 0:ei], lg[:, 0:ei], lg[:, ei:ei + 1],
                            ALU.is_equal))
                        P.ve(lambda h, ei=ei: h.reduce_sum(
                            cnt2, eqb[:, 0:ei], axis=AX.X))
                        P.ve(lambda h: h.tensor_add(cnt1, cnt1, cnt2))
                    m_last = P.ve(lambda h, ei=ei: h.tensor_single_scalar(
                        sel[:, ei:ei + 1], cnt1, 2.0, ALU.is_lt))
                P.ve(lambda h: h.tensor_mul(cw, el, sel))
                P.ve(lambda h: h.reduce_sum(cnt1, cw, axis=AX.X))
                P.ve(lambda h: h.reciprocal(cnt2, cnt1))
                m_cb = P.ve(lambda h, tc=tc: h.tensor_scalar(
                    A_f[:, ZC + tc * 8: ZC + (tc + 1) * 8], cw, cnt2, None,
                    ALU.mult))
            routing_done = ("ve", P.c["ve"])

            def gate_src(ex, ic):
                if ex < NE:
                    return io["l0_w_gate"][ex, :, ic * 128:(ic + 1) * 128]
                j = ex - NE
                return io["l0_sw_gate"][:, j * EI + ic * 128: j * EI + (ic + 1) * 128]

            def up_src(ex, ic):
                if ex < NE:
                    return io["l0_w_up"][ex, :, ic * 128:(ic + 1) * 128]
                j = ex - NE
                return io["l0_sw_up"][:, j * EI + ic * 128: j * EI + (ic + 1) * 128]

            def down_src(ex, oc):
                if ex < NE:
                    return io["l0_w_down"][ex, :, oc * 128:(oc + 1) * 128]
                j = ex - NE
                return io["l0_sw_down"][j * EI:(j + 1) * EI, oc * 128:(oc + 1) * 128]

            def load_sl(src2d, slot_cols, nkc, sem, waits=()):
                srcr = src2d.rearrange("(k p) m -> p k m", p=128)
                dst = A_w[:, slot_cols:slot_cols + nkc * 128].rearrange(
                    "p (k m) -> p k m", k=nkc)
                return P.dma("gp", sem, dst, srcr, waits=waits)

            class Pool:
                def __init__(self, slots):
                    self.slots = slots
                    self.cons = [None] * len(slots)
                    self.i = 0

                def acq(self):
                    j = self.i % len(self.slots)
                    self.i += 1
                    return j, self.slots[j], self.cons[j]

            pool = Pool([ps_a, ps_b, ps_w[:, 0:512], ps_w[:, 512:1024]])

            for half in range(2):
                bar = P.barrier()
                t0h = half * 1024

                c_xh = P.dma("sy", "xbh",
                             A_x[:, 0:16384].rearrange("p (k t) -> p k t", k=KC),
                             xb_s[:, t0h:t0h + 1024].rearrange(
                                 "(k p) t -> p k t", p=128))

                def xbh(k, q):
                    return A_x[:, k * 1024 + q * 512: k * 1024 + q * 512 + 512]

                def acct(oc, q):
                    c0 = 16384 + oc * 1024 + q * 512
                    return A_x[:, c0:c0 + 512]

                sgwb = A_t[:, 3584:3600]
                m_sgw = P.ve(lambda h: h.tensor_copy(sgwb, A_bias[:, 160:176]))
                srow = row(1024, 1024)
                m_sr = None
                for q in range(2):
                    tec = None
                    for k in range(KC):
                        tec = P.te(lambda h, k=k, q=q: h.matmul(
                            ps_a[0:1, 0:512], sgwb[:, k:k + 1], xbh(k, q),
                            start=(k == 0), stop=(k == KC - 1)),
                            waits=([("ve", m_sgw), ("xbh", c_xh), pool.cons[0],
                                    (m_sr and ("ve", m_sr))] if k == 0 else ()),
                            mark=(k == KC - 1))
                    m_sr = P.ve(lambda h, q=q: h.tensor_copy(
                        row(1024 + q * 512, 512), ps_a[0:1, 0:512]),
                        waits=[("te", tec)])
                pool.cons[0] = ("ve", m_sr)
                m_sig = P.sc(lambda h: h.activation(srow, srow, AF.Sigmoid,
                                                    bias=0.0, scale=1.0),
                             waits=[("ve", m_sr)])
                tb1 = P.te(lambda h: h.matmul(ps_v[:, 0:512], ones_rf[:],
                                              srow[0:1, 0:512], start=True,
                                              stop=True),
                           waits=[("sc", m_sig)])
                tb2 = P.te(lambda h: h.matmul(ps_v[:, 512:1024], ones_rf[:],
                                              srow[0:1, 512:1024], start=True,
                                              stop=True))
                sig_b = A_t[:, 1024:2048]
                m_sb = P.ve(lambda h: h.tensor_copy(sig_b, ps_v[:, 0:1024]),
                            waits=[("te", tb2)])
                psv_cons = ("ve", m_sb)

                wg_load = {}
                wu_load = {}
                wd_load = {}
                wg_slots = [0, 2048]
                wu_slots = [4096, 6144]
                wd_slots = [8192, 9600]
                wg_use = [None, None]
                wu_use = [None, None]
                wd_use = [None, None]
                cbc_cons = None
                hs_cons = {}
                acc_cons = {}
                sg_cons = [None, None]
                hm_cons = [None, ("te", P.c["te"])]
                GG = 0
                for ex in range(12):
                    if ex < NE:
                        crow = row(0, 1024)
                        trs = []
                        for tcq in range(8):
                            tc = half * 8 + tcq
                            col = ZC + tc * 8 + ex
                            m_tr = P.te(lambda h, col=col, tcq=tcq: h.transpose(
                                ps_v[0:1, tcq * 128:(tcq + 1) * 128],
                                A_f[:, col:col + 1], ident[:]),
                                waits=[routing_done, psv_cons]
                                if tcq == 0 else ())
                            trs.append(m_tr)
                        m_cr = P.ve(lambda h, crow=crow: h.tensor_copy(
                            crow, ps_v[0:1, 0:1024]), waits=[("te", trs[-1])])
                        b1 = P.te(lambda h, crow=crow: h.matmul(
                            ps_v[:, 0:512], ones_rf[:], crow[0:1, 0:512],
                            start=True, stop=True), waits=[("ve", m_cr)])
                        b2 = P.te(lambda h, crow=crow: h.matmul(
                            ps_v[:, 512:1024], ones_rf[:], crow[0:1, 512:1024],
                            start=True, stop=True))
                        m_cbc = P.ve(lambda h: h.tensor_copy(
                            A_t[:, 0:1024], ps_v[:, 0:1024]),
                            waits=[("te", b2), cbc_cons])
                        cbc_cons = ("ve", m_cbc)
                        psv_cons = ("ve", m_cbc)
                        cgate = A_t[:, 0:1024]
                        cb_ready = ("ve", m_cbc)
                    else:
                        cgate = sig_b
                        cb_ready = ("ve", m_sb)

                    for ic in range(EIC):
                        par = ic % 2
                        wg_load[par] = load_sl(gate_src(ex, ic), wg_slots[par],
                                               KC, "wg%d" % par,
                                               waits=[wg_use[par]])
                        wu_load[par] = load_sl(up_src(ex, ic), wu_slots[par],
                                               KC, "wu%d" % par,
                                               waits=[wu_use[par]])
                        for q in range(2):
                            jg, pg, cg = pool.acq()
                            ju, pu, cu = pool.acq()
                            tec = None
                            for k in range(KC):
                                tec = P.te(lambda h, k=k, q=q, pg=pg, par=par: h.matmul(
                                    pg[0:128, 0:512],
                                    A_w[:, wg_slots[par] + k * 128:
                                        wg_slots[par] + (k + 1) * 128],
                                    xbh(k, q), start=(k == 0), stop=(k == KC - 1)),
                                    waits=([("wg%d" % par, wg_load[par]),
                                            ("xbh", c_xh), cg] if k == 0 else ()),
                                    mark=(k == KC - 1))
                            tg = tec
                            for k in range(KC):
                                tec = P.te(lambda h, k=k, q=q, pu=pu, par=par: h.matmul(
                                    pu[0:128, 0:512],
                                    A_w[:, wu_slots[par] + k * 128:
                                        wu_slots[par] + (k + 1) * 128],
                                    xbh(k, q), start=(k == 0), stop=(k == KC - 1)),
                                    waits=([("wu%d" % par, wu_load[par]), cu]
                                           if k == 0 else ()),
                                    mark=(k == KC - 1))
                            tu = tec
                            gp2 = GG % 2
                            sg_t = A_t[:, 2048 + gp2 * 512: 2560 + gp2 * 512]
                            hm_t = A_t[:, 3072 + gp2 * 512: 3584 + gp2 * 512]
                            m_sg2 = P.sc(lambda h, sg_t=sg_t, pg=pg: h.activation(
                                sg_t, pg[0:128, 0:512], AF.Silu, bias=0.0,
                                scale=1.0),
                                waits=[("te", tg), sg_cons[gp2]])
                            m_hm = P.ve(lambda h, hm_t=hm_t, pu=pu, sg_t=sg_t:
                                        h.tensor_mul(hm_t, pu[0:128, 0:512], sg_t),
                                        waits=[("te", tu), ("sc", m_sg2),
                                               hm_cons[gp2]])
                            hdst = A_y[:, ic * 1024 + q * 512:
                                       ic * 1024 + q * 512 + 512]
                            m_h = P.ve(lambda h, hdst=hdst, hm_t=hm_t, cgate=cgate,
                                       q=q: h.tensor_mul(
                                           hdst, hm_t, cgate[:, q * 512:(q + 1) * 512]),
                                       waits=[cb_ready, hs_cons.get((ic, q))])
                            pool.cons[jg] = ("sc", m_sg2)
                            pool.cons[ju] = ("ve", m_hm)
                            sg_cons[gp2] = ("ve", m_hm)
                            hm_cons[gp2] = ("ve", m_h)
                            hs_cons[(ic, q)] = ("ve", m_h)
                            GG += 1
                        wg_use[par] = ("te", tg)
                        wu_use[par] = ("te", tu)

                    for oc in range(KC):
                        par = oc % 2
                        wd_load[par] = load_sl(down_src(ex, oc), wd_slots[par],
                                               EIC, "wd%d" % par,
                                               waits=[wd_use[par]])
                        for q in range(2):
                            jd, pd, cd = pool.acq()
                            tec = None
                            for k in range(EIC):
                                tec = P.te(lambda h, k=k, q=q, pd=pd, par=par: h.matmul(
                                    pd[0:128, 0:512],
                                    A_w[:, wd_slots[par] + k * 128:
                                        wd_slots[par] + (k + 1) * 128],
                                    A_y[:, k * 1024 + q * 512:
                                        k * 1024 + q * 512 + 512],
                                    start=(k == 0), stop=(k == EIC - 1)),
                                    waits=([("wd%d" % par, wd_load[par]), cd,
                                            hs_cons.get((k, q))] if k == 0
                                           else [hs_cons.get((k, q))]),
                                    mark=(k == EIC - 1))
                            if ex == 0:
                                m_acc = P.ve(lambda h, oc=oc, q=q, pd=pd:
                                             h.tensor_copy(acct(oc, q),
                                                           pd[0:128, 0:512]),
                                             waits=[("te", tec),
                                                    acc_cons.get((oc, q)),
                                                    ("xbh", c_xh)])
                            else:
                                m_acc = P.ve(lambda h, oc=oc, q=q, pd=pd:
                                             h.tensor_add(acct(oc, q),
                                                          pd[0:128, 0:512],
                                                          acct(oc, q)),
                                             waits=[("te", tec),
                                                    acc_cons.get((oc, q))])
                            pool.cons[jd] = ("ve", m_acc)
                            acc_cons[(oc, q)] = ("ve", m_acc)
                        wd_use[par] = ("te", tec)

                pl = PostLN(tok0=t0h, ntoks=1024,
                            init_guard=[acc_cons[(15, 1)],
                                        pool.cons[2], pool.cons[3]])
                pl.row_pair_cons = [pool.cons[2], pool.cons[3]]
                xpf = {"load": {}, "use": {}}
                guard0 = acc_cons[(15, 1)]

                def xload(G):
                    if G in xpf["load"] or G >= 32:
                        return
                    s4 = G % 4
                    oc2, nt2 = G // 2, G % 2
                    xpf["load"][G] = P.dma(
                        "sy", "ox%d" % s4, zb(s4),
                        x_src[oc2 * 128:(oc2 + 1) * 128,
                              t0h + nt2 * 512: t0h + nt2 * 512 + 512],
                        waits=[xpf["use"].get(G - 4, guard0)])

                for oc in range(KC):
                    for nt in range(2):
                        G = pl.G
                        for gg in (G, G + 1, G + 2):
                            xload(gg)
                        s4 = G % 4
                        g2 = G % 2
                        m2 = P.ve(lambda h, oc=oc, nt=nt, s4=s4, g2=g2:
                                  h.tensor_add(st(g2), acct(oc, nt), zb(s4)),
                                  waits=[("ox%d" % s4, xpf["load"][G]),
                                         acc_cons[(oc, nt)],
                                         pl.st_slot_wait()])
                        xpf["use"][G] = ("ve", m2)
                        pl.chunk(oc, nt, ("ve", m2))
                pl.passB(dst_f32, dst_bf)

        def emit_l1ffn_prep():
            use = [None, None]
            out = [None, None]
            idx = 0
            for (wsrc, wdst, nrows) in [(io["l1_wqgT"], wg_bf, KC),
                                        (io["l1_wquT"], wu_bf, KC),
                                        (io["l1_wqdT"], wd_bf, ISC)]:
                ncols = IS if nrows == KC else D
                cb_w = 1408 if nrows == KC else 2048
                for rr in range(nrows):
                    for cb in range(ncols // cb_w):
                        par = idx % 2
                        raw = A_y[:, par * 2048: par * 2048 + cb_w]
                        sub = A_y[:, 4096 + par * 2048: 4096 + par * 2048 + cb_w]
                        c_c = P.dma("gp", "dqi%d" % par, raw,
                                    wsrc[rr * 128:(rr + 1) * 128,
                                         cb * cb_w:(cb + 1) * cb_w],
                                    waits=[use[par]])
                        m_v = P.ve(lambda h, sub=sub, raw=raw: h.tensor_scalar_add(
                            sub, raw, -128.0),
                            waits=[("dqi%d" % par, c_c), out[par]])
                        c_o = P.dma("sy", "dqo%d" % par,
                                    wdst[rr * 128:(rr + 1) * 128,
                                         cb * cb_w:(cb + 1) * cb_w], sub,
                                    waits=[("ve", m_v)])
                        use[par] = ("ve", m_v)
                        out[par] = ("dqo%d" % par, c_o)
                        idx += 1
            return [out[0], out[1]]

        def emit_l1_attn(x_src, dst_f32, dst_bf, do_prep=False):
            P.barrier()
            c_xb = load_xb_full([])
            rhs_ready = [("xbld", c_xb)]

            def phi_post():
                state = {"zb": [None, None], "e3": None, "st": [None, None],
                         "pb": [None, None], "pt": None, "psv": None}

                def post(oc, nt, ps, tec):
                    par = (oc * NT + nt) % 2
                    t_raw = zb(par)
                    tm = zb(2)
                    te_ = zb(3)
                    m1 = P.sc(lambda h, t_raw=t_raw, ps=ps, oc=oc: h.activation(
                        t_raw, ps[:], AF.Identity, bias=bias_ap("l1_bqk", oc),
                        scale=1.0), waits=[("te", tec), state["zb"][par]])
                    m_tm = P.ve(lambda h, tm=tm, t_raw=t_raw: h.tensor_scalar_min(
                        tm, t_raw, 0.0), waits=[("sc", m1), state["e3"]])
                    m_e = P.sc(lambda h, te_=te_, tm=tm: h.activation(
                        te_, tm, AF.Exp, bias=0.0, scale=1.0),
                        waits=[("ve", m_tm)])
                    m_tr = P.ve(lambda h, tm=tm, t_raw=t_raw: h.tensor_scalar_max(
                        tm, t_raw, 0.0), waits=[("sc", m_e)])
                    phif = st(par)
                    m_f = P.ve(lambda h, phif=phif, te_=te_, tm=tm:
                               h.scalar_tensor_tensor(phif, te_, -1.0, tm,
                                                      ALU.add, ALU.add),
                               waits=[state["st"][par]])
                    state["e3"] = ("ve", m_f)
                    state["zb"][par] = ("ve", m_f)
                    phib = A_t[:, 2048 + par * 512: 2560 + par * 512]
                    m_b = P.sc(lambda h, phib=phib, phif=phif: h.copy(phib, phif),
                               waits=[("ve", m_f), state["pb"][par]])
                    dst = phi_f[oc * 128:(oc + 1) * 128, nt * 512:(nt + 1) * 512]
                    m_d1 = P.dma("sy", "phif", dst, phib, waits=[("sc", m_b)])
                    state["pb"][par] = ("phif", m_d1)
                    trs = None
                    for i in range(4):
                        trs = P.te(lambda h, i=i, phif=phif: h.transpose(
                            ps_v[:, i * 128:(i + 1) * 128],
                            phif[:, i * 128:(i + 1) * 128], ident[:]),
                            waits=([("ve", m_f), state["psv"]] if i == 0 else ()))
                    ptb = A_t[:, 3072 + par * 512: 3584 + par * 512]
                    m_pt = P.ve(lambda h, ptb=ptb: h.tensor_copy(
                        ptb, ps_v[:, 0:512]), waits=[("te", trs), state["pt"]])
                    state["psv"] = ("ve", m_pt)
                    dstt = phi_t[nt * 512:(nt + 1) * 512,
                                 oc * 128:(oc + 1) * 128].rearrange(
                        "(i p) d -> p i d", p=128)
                    m_d2 = P.dma("sy", "phit", dstt,
                                 ptb.rearrange("p (i d) -> p i d", i=4),
                                 waits=[("ve", m_pt)])
                    state["pt"] = ("phit", m_d2)
                    state["st"][par] = [("sc", m_b), ("te", trs)]
                    return ("sc", m1)
                return post

            phi_cons = proj_like(io["l1_wqk"], xb_tile, phi_post(),
                                 rhs_ready=rhs_ready,
                                 w_waits=[("te", P.c["te"])])

            bvrow = row(0, D)
            c_bv = P.dma("sy", "misc", bvrow, io["l1_bv_row"])
            tb = bcast_row_f32(bvrow, D, 0, waits=[("misc", c_bv),
                                                   phi_cons[0], phi_cons[1]])
            m_bvb = P.ve(lambda h: h.tensor_copy(A_f[:, ZB:ZB + 2048],
                                                 ps_w[:, 0:2048]),
                         waits=[("te", tb)])
            phidone = [("phif", P.d.get("phif", 0)), ("phit", P.d.get("phit", 0))]
            state_v = {"cons": [phi_cons[0], phi_cons[1]],
                       "slot": [phidone, phidone], "f": [None, None]}
            slotv = [0, 8192]
            te_pre = P.c["te"]
            wloadv = [load_wslice(io["l1_wv"], 0, 512, slotv[0], KC, "w0",
                                  waits=[("te", te_pre)]),
                      load_wslice(io["l1_wv"], 512, 512, slotv[1], KC, "w1",
                                  waits=[("te", te_pre)])]
            Gv = 0
            for ovb in range(4):
                blk_te = None
                for tc in range(KC):
                    par = Gv % 2
                    ps = ps_a if par == 0 else ps_b
                    wts = [(("w0" if ovb % 2 == 0 else "w1"), wloadv[ovb % 2]),
                           ("xbld", c_xb), state_v["cons"][par]]
                    tec = None
                    for k in range(KC):
                        lhsT = A_x[:, k * TC + tc * 128: k * TC + tc * 128 + 128]
                        rt = A_w[:, slotv[ovb % 2] + k * 512:
                                 slotv[ovb % 2] + (k + 1) * 512]
                        tec = P.te(lambda h, ps=ps, lhsT=lhsT, rt=rt,
                                   st_=(k == 0), sp=(k == KC - 1): h.matmul(
                                       ps[:], lhsT, rt, start=st_, stop=sp),
                                   waits=(wts if k == 0 else ()),
                                   mark=(k == KC - 1))
                    blk_te = tec
                    vf = st(par)
                    mv = P.ve(lambda h, vf=vf, ps=ps, ovb=ovb: h.tensor_add(
                        vf, ps[:], A_f[:, ZB + ovb * 512: ZB + (ovb + 1) * 512]),
                        waits=[("te", tec), ("ve", m_bvb), state_v["f"][par]])
                    vout = A_t[:, 2048 + par * 512: 2560 + par * 512]
                    ms = P.sc(lambda h, vout=vout, vf=vf: h.activation(
                        vout, vf, AF.Silu, bias=0.0, scale=1.0),
                        waits=[("ve", mv), state_v["slot"][par]])
                    dst = v_t[tc * 128:(tc + 1) * 128, ovb * 512:(ovb + 1) * 512]
                    md = P.dma("sy", "vto%d" % par, dst, vout, waits=[("sc", ms)])
                    state_v["cons"][par] = ("ve", mv)
                    state_v["f"][par] = ("sc", ms)
                    state_v["slot"][par] = ("vto%d" % par, md)
                    Gv += 1
                if ovb + 2 < 4:
                    wloadv[ovb % 2] = load_wslice(
                        io["l1_wv"], (ovb + 2) * 512, 512, slotv[ovb % 2], KC,
                        "w0" if ovb % 2 == 0 else "w1", waits=[("te", blk_te)])

            def u_post(oc, nt, ps, tec):
                par = (oc * NT + nt) % 2
                ub = A_t[:, 3072 + par * 512: 3584 + par * 512]
                m1 = P.sc(lambda h, ub=ub, ps=ps, oc=oc: h.activation(
                    ub, ps[:], AF.Silu, bias=bias_ap("l1_bu", oc), scale=1.0),
                    waits=[("te", tec), u_state["slot"][par]])
                dst = u_f[oc * 128:(oc + 1) * 128, nt * 512:(nt + 1) * 512]
                md = P.dma("sy", "ufo%d" % par, dst, ub, waits=[("sc", m1)])
                u_state["slot"][par] = ("ufo%d" % par, md)
                return ("sc", m1)

            u_state = {"slot": [phidone, phidone]}
            u_cons = proj_like(io["l1_wu"], xb_tile, u_post,
                               rhs_ready=rhs_ready,
                               w_waits=[("te", P.c["te"])],
                               ps_init=(state_v["cons"][0], state_v["cons"][1]))
            P.dq_done = emit_l1ffn_prep() if do_prep else []

            snapKV = P.snap()
            P.join("gp", snapKV)
            kv_done = None
            tl = {0: None, 1: None}
            vl = {0: None, 1: None}
            kvw = [("te", P.c["te"]), u_cons[0], u_cons[1],
                   ("phit", P.d.get("phit", 0)),
                   ("vto0", P.d.get("vto0", 0)), ("vto1", P.d.get("vto1", 0))]
            banks = [ps_a[:, 0:128], ps_b[:, 0:128], ps_w[:, 0:128],
                     ps_w[:, 512:640], ps_w[:, 1024:1152], ps_w[:, 1536:1664],
                     ps_v[:, 0:128], ps_v[:, 512:640]]
            kvsb = A_f[:, ZB:ZB + 2048]
            m_kv = None
            tluse = [None, None]
            for pas in range(2):
                tec = None
                for tc in range(KC):
                    par = tc % 2
                    tl[par] = P.dma("sy", "fk%d" % par,
                                    A_w[:, par * 2048:(par + 1) * 2048],
                                    phi_t[tc * 128:(tc + 1) * 128, :],
                                    waits=kvw + [tluse[par]])
                    vl[par] = P.dma("sy", "fv%d" % par,
                                    A_w[:, 4096 + par * 2048:
                                        4096 + (par + 1) * 2048],
                                    v_t[tc * 128:(tc + 1) * 128, :],
                                    waits=kvw + [tluse[par]])
                    for hb in range(8):
                        h_ = pas * 8 + hb
                        tec = P.te(lambda h, h_=h_, hb=hb, par=par, tc=tc: h.matmul(
                            banks[hb],
                            A_w[:, par * 2048 + h_ * 128:
                                par * 2048 + (h_ + 1) * 128],
                            A_w[:, 4096 + par * 2048 + h_ * 128:
                                4096 + par * 2048 + (h_ + 1) * 128],
                            start=(tc == 0), stop=(tc == KC - 1)),
                            waits=([("fk%d" % par, tl[par]),
                                    ("fv%d" % par, vl[par]),
                                    (m_kv and ("ve", m_kv))]
                                   if hb == 0 else ()),
                            mark=(hb == 7))
                    tluse[par] = ("te", tec)
                    kvw = []
                for hb in range(8):
                    h_ = pas * 8 + hb
                    m_kv = P.ve(lambda h, h_=h_, hb=hb: h.tensor_copy(
                        kvsb[:, h_ * 128:(h_ + 1) * 128], banks[hb]),
                        waits=[("te", tec)] if hb == 0 else [])
            c_kvout = P.dma("sy", "misc", kv_c, kvsb, waits=[("ve", m_kv)])
            P.emit("gp", lambda h: None, waits=[("misc", P.d.get("misc", 0))])
            c_cc2 = P.emit("gp", lambda h: h.collective_compute(
                "AllGather", ALU.bypass, replica_groups=GROUPS4,
                ins=[kv_c.opt()], outs=[kv_g.opt()]), inc="cc")
            prev = None
            for r in range(4):
                for hh in range(4):
                    g2 = hh % 2
                    c_l = P.dma("sy", "rl%d" % g2, st(g2),
                                kv_g[r * 128:(r + 1) * 128,
                                     hh * 512:(hh + 1) * 512],
                                waits=[("cc", c_cc2), prev])
                    if r == 0:
                        m_p = P.ve(lambda h, hh=hh, g2=g2: h.tensor_copy(
                            kvsb[:, hh * 512:(hh + 1) * 512], st(g2)),
                            waits=[("rl%d" % g2, c_l), ("ve", m_kv),
                                   ("misc", c_kvout)])
                    else:
                        m_p = P.ve(lambda h, hh=hh, g2=g2: h.tensor_add(
                            kvsb[:, hh * 512:(hh + 1) * 512], st(g2),
                            kvsb[:, hh * 512:(hh + 1) * 512]),
                            waits=[("rl%d" % g2, c_l)])
                    prev = ("ve", m_p)
            kv_bf = A_t[:, 0:2048]
            m_kvb = P.sc(lambda h: h.copy(kv_bf, kvsb), waits=[prev])

            pf = {0: None, 1: None}
            pfuse = [("te", P.c["te"]), ("te", P.c["te"])]
            ln_cons = None
            ps_cons = [u_cons[0], u_cons[1]]
            GG = 0
            for h_ in range(H):
                par = h_ % 2
                pf[par] = P.dma("sy", "fk%d" % par,
                                A_w[:, par * 2048:(par + 1) * 2048],
                                phi_f[h_ * 128:(h_ + 1) * 128, :],
                                waits=[pfuse[par],
                                       ("phif", P.d.get("phif", 0))])
                for nt in range(NT):
                    j = GG % 2
                    ps = ps_a if j == 0 else ps_b
                    tec = P.te(lambda h, h_=h_, nt=nt, ps=ps, par=par: h.matmul(
                        ps[:], kv_bf[:, h_ * 128:(h_ + 1) * 128],
                        A_w[:, par * 2048 + nt * 512: par * 2048 + nt * 512 + 512],
                        start=True, stop=True),
                        waits=[("sc", m_kvb), ("fk%d" % par, pf[par]),
                               ps_cons[j]])
                    m_e = P.ve(lambda h, h_=h_, nt=nt, ps=ps: h.tensor_copy(
                        A_y[:, h_ * 2048 + nt * 512: h_ * 2048 + nt * 512 + 512],
                        ps[:]),
                        waits=[("te", tec)] + (P.dq_done if GG == 0 else []))
                    ps_cons[j] = ("ve", m_e)
                    ln_cons = ("ve", m_e)
                    pfuse[par] = ("te", tec)
                    GG += 1

            pli = PostLN(init_guard=ln_cons)
            pli.row_pair_cons = [ln_cons, ln_cons]
            uin = [None, None]
            zcons = {}
            for nt in range(NT):
                for hc in range(KC):
                    lnt = A_y[:, hc * 2048 + nt * 512: hc * 2048 + nt * 512 + 512]
                    g2 = pli.G % 2
                    rsq = at(2)
                    m_rsq = P.ve(lambda h, rsq=rsq, lnt=lnt: h.tensor_mul(
                        rsq, lnt, lnt), waits=[ln_cons, pli.rsq_guard()])
                    ssum = ps_w[0:1, g2 * 1024: g2 * 1024 + 512]
                    ssq = ps_w[0:1, g2 * 1024 + 512: g2 * 1024 + 1024]
                    m_t1 = P.te(lambda h, ssum=ssum, lnt=lnt: h.matmul(
                        ssum, ones_cb[:], lnt, start=True, stop=True),
                        waits=[ln_cons, pli.row_pair_cons[g2]])
                    m_t2 = P.te(lambda h, ssq=ssq, rsq=rsq: h.matmul(
                        ssq, ones_cb[:], rsq, start=True, stop=True),
                        waits=[("ve", m_rsq)])
                    pli._last_t2 = m_t2
                    sr = row(nt * 512, 512)
                    qr = row(2048 + nt * 512, 512)
                    if pli.first[nt]:
                        m_a1 = P.ve(lambda h, sr=sr, ssum=ssum: h.tensor_copy(
                            sr, ssum), waits=[("te", m_t1)])
                        m_a2 = P.ve(lambda h, qr=qr, ssq=ssq: h.tensor_copy(
                            qr, ssq), waits=[("te", m_t2)])
                        pli.first[nt] = False
                    else:
                        m_a1 = P.ve(lambda h, sr=sr, ssum=ssum: h.tensor_add(
                            sr, ssum, sr), waits=[("te", m_t1)])
                        m_a2 = P.ve(lambda h, qr=qr, ssq=ssq: h.tensor_add(
                            qr, ssq, qr), waits=[("te", m_t2)])
                    pli.row_pair_cons[g2] = ("ve", m_a2)
                    pli.G += 1
                    pli.a_done = ("ve", m_a2)
            stats_done = pli.a_done
            for nt in range(NT):
                sr = row(nt * 512, 512)
                qr = row(2048 + nt * 512, 512)
                tr = A_f[0:1, ZB + 1536: ZB + 2048]
                P.ve(lambda h, sr=sr: h.tensor_scalar(sr, sr, 1.0 / D, None,
                                                      ALU.mult),
                     waits=[stats_done])
                P.ve(lambda h, qr=qr: h.tensor_scalar(qr, qr, 1.0 / D, None,
                                                      ALU.mult))
                P.ve(lambda h, tr=tr, sr=sr: h.tensor_mul(tr, sr, sr))
                m4 = P.ve(lambda h, qr=qr, tr=tr: h.scalar_tensor_tensor(
                    qr, qr, LN_EPS, tr, ALU.add, ALU.subtract))
                m5 = P.sc(lambda h, tr=tr, qr=qr: h.activation(
                    tr, qr, AF.Sqrt, bias=0.0, scale=1.0), waits=[("ve", m4)])
                m6 = P.ve(lambda h, qr=qr, tr=tr: h.reciprocal(qr, tr),
                          waits=[("sc", m5)])
                b1 = bcast_row_f32(sr, 512, 0, waits=[("ve", m6)])
                b2 = bcast_row_f32(qr, 512, 512)
                mub, rsb = zb(0), zb(1)
                e1 = P.ve(lambda h, mub=mub: h.tensor_copy(mub, ps_w[:, 0:512]),
                          waits=[("te", b1)])
                e2 = P.ve(lambda h, rsb=rsb: h.tensor_copy(rsb,
                                                           ps_w[:, 512:1024]),
                          waits=[("te", b2)])
                for hc in range(KC):
                    g2 = hc % 2
                    lnt = A_y[:, hc * 2048 + nt * 512: hc * 2048 + nt * 512 + 512]
                    t1 = zb(2)
                    t2v = zb(3)
                    P.ve(lambda h, t1=t1, lnt=lnt, mub=mub: h.tensor_sub(
                        t1, lnt, mub), waits=[("ve", e2)])
                    m_t2 = P.ve(lambda h, t2v=t2v, t1=t1, rsb=rsb: h.tensor_mul(
                        t2v, t1, rsb))
                    aff = st(g2)
                    m_af = P.sc(lambda h, aff=aff, t2v=t2v, hc=hc: h.activation(
                        aff, t2v, AF.Identity, bias=bias_ap("l1_lnb", hc),
                        scale=A_bias[:, BOFF["l1_lng"] + hc:
                                     BOFF["l1_lng"] + hc + 1]),
                        waits=[("ve", m_t2), zcons.get(("st", g2))])
                    ut = A_t[:, 2048 + g2 * 512: 2560 + g2 * 512]
                    c_u = P.dma("sy", "ufi%d" % g2, ut,
                                u_f[hc * 128:(hc + 1) * 128,
                                    nt * 512:(nt + 1) * 512],
                                waits=[("ufo0", P.d.get("ufo0", 0)),
                                       ("ufo1", P.d.get("ufo1", 0)),
                                       zcons.get(("ut", g2))])
                    m_z = P.ve(lambda h, hc=hc, nt=nt, aff=aff, ut=ut:
                               h.tensor_mul(
                                   A_x[:, hc * TC + nt * 512:
                                       hc * TC + nt * 512 + 512], aff, ut),
                               waits=[("sc", m_af), ("ufi%d" % g2, c_u),
                                      zcons.get(("z", hc, nt))])
                    zcons[("st", g2)] = ("ve", m_z)
                    zcons[("ut", g2)] = ("ve", m_z)
                    zcons[("z", hc, nt)] = ("ve", m_z)
            z_done = ("ve", P.c["ve"])

            pl = PostLN(init_guard=z_done)
            xpf = {"load": {}, "use": {}}

            def xload(G):
                if G in xpf["load"] or G >= 64:
                    return
                s4 = G % 4
                oc2, nt2 = G // NT, G % NT
                xpf["load"][G] = P.dma(
                    "sy", "ox%d" % s4, zb(s4),
                    x_src[oc2 * 128:(oc2 + 1) * 128,
                          nt2 * 512:(nt2 + 1) * 512],
                    waits=[xpf["use"].get(G - 4, z_done)])

            def oproj_post(oc, nt, ps, tec):
                G = pl.G
                for gg in (G, G + 1, G + 2):
                    xload(gg)
                s4 = G % 4
                g2 = G % 2
                m2 = P.ve(lambda h, ps=ps, s4=s4, g2=g2, oc=oc:
                          h.scalar_tensor_tensor(
                              st(g2), ps[:], bias_ap("l1_bo", oc), zb(s4),
                              ALU.add, ALU.add),
                          waits=[("te", tec), ("ox%d" % s4, xpf["load"][G]),
                                 pl.st_slot_wait()])
                xpf["use"][G] = ("ve", m2)
                return pl.chunk(oc, nt, ("ve", m2))

            proj_like(io["l1_wo"], xb_tile, oproj_post,
                      rhs_ready=[z_done], w_waits=[("te", P.c["te"])],
                      ps_init=(ps_cons[0], ps_cons[1]))
            pl.passB(dst_f32, dst_bf)

        def emit_l1_ffn(x_src, dst_f32, dst_bf):
            P.barrier()
            dq_done = P.dq_done
            class Pool:
                def __init__(self, slots):
                    self.slots = slots
                    self.cons = [None] * len(slots)
                    self.i = 0

                def acq(self):
                    j = self.i % len(self.slots)
                    self.i += 1
                    return j, self.slots[j], self.cons[j]

            pool = Pool([ps_a, ps_b, ps_w[:, 0:512], ps_w[:, 512:1024]])

            def load_sl(src2d, slot_cols, nkc, sem, waits=()):
                srcr = src2d.rearrange("(k p) m -> p k m", p=128)
                dst = A_w[:, slot_cols:slot_cols + nkc * 128].rearrange(
                    "p (k m) -> p k m", k=nkc)
                return P.dma("gp", sem, dst, srcr, waits=waits)

            q_te_prev = ("te", P.c["te"])
            q_ve_prev = ("ve", P.c["ve"])
            P.barrier()
            for qq in range(NT):
                t0q = qq * 512
                c_xq = P.dma("sy", "xbh",
                             A_x[:, 0:8192].rearrange("p (k t) -> p k t", k=KC),
                             xb_s[:, t0q:t0q + 512].rearrange(
                                 "(k p) t -> p k t", p=128),
                             waits=[q_te_prev])

                def xbq(k):
                    return A_x[:, k * 512:(k + 1) * 512]

                wg_load = [None, None]
                wu_load = [None, None]
                wg_use = [q_te_prev, q_te_prev]
                wu_use = [q_te_prev, q_te_prev]
                sg_cons = [q_ve_prev, q_ve_prev]
                h_cons = {hc: q_te_prev for hc in range(ISC)}
                wg_slots = [0, 2048]
                wu_slots = [4096, 6144]
                for hc in range(ISC):
                    par = hc % 2
                    wg_load[par] = load_sl(wg_bf[:, hc * 128:(hc + 1) * 128],
                                           wg_slots[par], KC, "wg%d" % par,
                                           waits=[wg_use[par], dq_done])
                    wu_load[par] = load_sl(wu_bf[:, hc * 128:(hc + 1) * 128],
                                           wu_slots[par], KC, "wu%d" % par,
                                           waits=[wu_use[par], dq_done])
                    jg, pg, cg = pool.acq()
                    ju, pu, cu = pool.acq()
                    tec = None
                    for k in range(KC):
                        tec = P.te(lambda h, k=k, pg=pg, par=par: h.matmul(
                            pg[0:128, 0:512],
                            A_w[:, wg_slots[par] + k * 128:
                                wg_slots[par] + (k + 1) * 128],
                            xbq(k), start=(k == 0), stop=(k == KC - 1)),
                            waits=([("wg%d" % par, wg_load[par]),
                                    ("xbh", c_xq), cg] if k == 0 else ()),
                            mark=(k == KC - 1))
                    tg = tec
                    for k in range(KC):
                        tec = P.te(lambda h, k=k, pu=pu, par=par: h.matmul(
                            pu[0:128, 0:512],
                            A_w[:, wu_slots[par] + k * 128:
                                wu_slots[par] + (k + 1) * 128],
                            xbq(k), start=(k == 0), stop=(k == KC - 1)),
                            waits=([("wu%d" % par, wu_load[par]), cu]
                                   if k == 0 else ()),
                            mark=(k == KC - 1))
                    tu = tec
                    gp2 = hc % 2
                    sg_t = A_t[:, 2048 + gp2 * 512: 2560 + gp2 * 512]
                    us_t = A_t[:, 3072 + gp2 * 512: 3584 + gp2 * 512]
                    m_sg2 = P.sc(lambda h, sg_t=sg_t, pg=pg, hc=hc: h.activation(
                        sg_t, pg[0:128, 0:512], AF.Silu, bias=0.0,
                        scale=A_bias[:, BOFF["l1_sg"] + hc:
                                     BOFF["l1_sg"] + hc + 1]),
                        waits=[("te", tg), sg_cons[gp2]])
                    m_us = P.sc(lambda h, us_t=us_t, pu=pu, hc=hc: h.activation(
                        us_t, pu[0:128, 0:512], AF.Identity, bias=0.0,
                        scale=A_bias[:, BOFF["l1_su"] + hc:
                                     BOFF["l1_su"] + hc + 1]),
                        waits=[("te", tu)])
                    m_h = P.ve(lambda h, hc=hc, sg_t=sg_t, us_t=us_t:
                               h.tensor_mul(A_y[:, hc * 512:(hc + 1) * 512],
                                            us_t, sg_t),
                               waits=[("sc", m_us), h_cons.get(hc)])
                    pool.cons[jg] = ("ve", m_h)
                    pool.cons[ju] = ("ve", m_h)
                    sg_cons[gp2] = ("ve", m_h)
                    h_cons[hc] = ("ve", m_h)
                    wg_use[par] = ("te", tg)
                    wu_use[par] = ("te", tu)

                pl = PostLN(tok0=t0q, ntoks=512,
                            init_guard=[pool.cons[2], pool.cons[3],
                                        ("yo0", P.d.get("yo0", 0)),
                                        ("yo1", P.d.get("yo1", 0)),
                                        ("ybo", P.d.get("ybo", 0))])
                pl.row_pair_cons = [pool.cons[2], pool.cons[3]]
                dpool = Pool([ps_a, ps_b])
                dpool.cons = [pool.cons[0], pool.cons[1]]
                wd_load = [None, None]
                wd_use = [("te", P.c["te"]), ("te", P.c["te"])]
                wd_slots = [0, 5632]
                xpf = {"load": {}, "use": {}}
                guard0 = [pool.cons[2], pool.cons[3], q_ve_prev]

                def xload(G, t0q=t0q):
                    if G in xpf["load"] or G >= KC:
                        return
                    s4 = G % 4
                    xpf["load"][G] = P.dma(
                        "sy", "ox%d" % s4, zb(s4),
                        x_src[G * 128:(G + 1) * 128, t0q:t0q + 512],
                        waits=[xpf["use"].get(G - 4, guard0)])

                for oc in range(KC):
                    par = oc % 2
                    wd_load[par] = load_sl(wd_bf[:, oc * 128:(oc + 1) * 128],
                                           wd_slots[par], ISC, "wd%d" % par,
                                           waits=[wd_use[par], dq_done])
                    jd, pd, cd = dpool.acq()
                    tec = None
                    for k in range(ISC):
                        tec = P.te(lambda h, k=k, pd=pd, par=par: h.matmul(
                            pd[0:128, 0:512],
                            A_w[:, wd_slots[par] + k * 128:
                                wd_slots[par] + (k + 1) * 128],
                            A_y[:, k * 512:(k + 1) * 512],
                            start=(k == 0), stop=(k == ISC - 1)),
                            waits=([("wd%d" % par, wd_load[par]), cd,
                                    h_cons.get(k)] if k == 0
                                   else [h_cons.get(k)]),
                            mark=(k == ISC - 1))
                    wd_use[par] = ("te", tec)
                    G = pl.G
                    for gg in (G, G + 1, G + 2):
                        xload(gg)
                    s4 = G % 4
                    g2 = G % 2
                    m2 = P.ve(lambda h, pd=pd, s4=s4, g2=g2, oc=oc:
                              h.scalar_tensor_tensor(
                                  st(g2), pd[0:128, 0:512],
                                  A_bias[:, BOFF["l1_sd"] + oc:
                                         BOFF["l1_sd"] + oc + 1], zb(s4),
                                  ALU.mult, ALU.add),
                              waits=[("te", tec), ("ox%d" % s4, xpf["load"][G]),
                                     pl.st_slot_wait()])
                    xpf["use"][G] = ("ve", m2)
                    dpool.cons[jd] = ("ve", m2)
                    pl.chunk(oc, 0, ("ve", m2))
                pool.cons[0] = dpool.cons[0]
                pool.cons[1] = dpool.cons[1]
                pl.passB(dst_f32, dst_bf)
                q_te_prev = ("te", P.c["te"])
                q_ve_prev = ("ve", P.c["ve"])
                pool.cons = [[c, q_ve_prev] if c is not None else q_ve_prev
                             for c in pool.cons]

        blocks = []

        def run_blocks():
            emit_l0_attn(io["x0T"], yT if upto == 1 else xfA, xb_s)
            if upto >= 2:
                emit_l0_ffn(xfA, yT if upto == 2 else xfB, xb_s)
            if upto >= 3:
                emit_l1_attn(xfB, yT if upto == 3 else xfA, xb_s,
                             do_prep=(upto >= 4))
            if upto >= 4:
                emit_l1_ffn(xfA, yT, None)

        run_blocks()
        snapF = P.snap()
        P.emit("sy", lambda h: None,
               waits=tuple((s, v) for s, v in snapF.items() if v > 0))
        P.lower(nc, sems, mksem)

    return nc, io


def stage_inputs(inputs):
    asc = np.ascontiguousarray
    x = np.asarray(inputs["L0_attn_x"], dtype=np.float32)
    xflat = x.reshape(NCORES * TC, D)

    def arr_fm(v):
        return asc(np.asarray(v, np.float32).reshape(KC, 128).T)

    shared = {
        "l0_wqT": asc(np.asarray(inputs["L0_attn_Wq"], np.float32).T),
        "l0_wkT": asc(np.asarray(inputs["L0_attn_Wk"], np.float32).T),
        "l0_wvT": asc(np.asarray(inputs["L0_attn_Wv"], np.float32).T),
        "l0_woT": asc(np.asarray(inputs["L0_attn_Wo"], np.float32).T),
        "l0_bq": arr_fm(inputs["L0_attn_bq"]),
        "l0_bk": arr_fm(inputs["L0_attn_bk"]),
        "l0_bo": arr_fm(inputs["L0_attn_bo"]),
        "l0_bv_row": asc(np.asarray(inputs["L0_attn_bv"], np.float32).reshape(1, D)),
        "l0_gate_wT": asc(np.asarray(inputs["L0_ffn_gate_w"], np.float32).T),
        "l0_w_gate": asc(np.asarray(inputs["L0_ffn_w_gate"], np.float32)),
        "l0_w_up": asc(np.asarray(inputs["L0_ffn_w_up"], np.float32)),
        "l0_w_down": asc(np.asarray(inputs["L0_ffn_w_down"], np.float32)),
        "l0_sw_gate": asc(np.asarray(inputs["L0_ffn_sw_gate"], np.float32)),
        "l0_sw_up": asc(np.asarray(inputs["L0_ffn_sw_up"], np.float32)),
        "l0_sw_down": asc(np.asarray(inputs["L0_ffn_sw_down"], np.float32)),
        "l0_sgw": arr_fm(np.asarray(inputs["L0_ffn_shared_gate_w"]).reshape(-1)),
        "l1_wqk": asc(np.asarray(inputs["L1_attn_Wqk"], np.float32)),
        "l1_wv": asc(np.asarray(inputs["L1_attn_Wv"], np.float32)),
        "l1_wu": asc(np.asarray(inputs["L1_attn_Wu"], np.float32)),
        "l1_wo": asc(np.asarray(inputs["L1_attn_Wo"], np.float32)),
        "l1_bqk": arr_fm(inputs["L1_attn_bqk"]),
        "l1_bu": arr_fm(inputs["L1_attn_bu"]),
        "l1_bo": arr_fm(inputs["L1_attn_bo"]),
        "l1_lng": arr_fm(inputs["L1_attn_ln_g"]),
        "l1_lnb": arr_fm(inputs["L1_attn_ln_b"]),
        "l1_bv_row": asc(np.asarray(inputs["L1_attn_bv"], np.float32).reshape(1, D)),
        "l1_wqgT": asc(np.asarray(inputs["L1_ffn_wq_gate"], np.int32).T),
        "l1_wquT": asc(np.asarray(inputs["L1_ffn_wq_up"], np.int32).T),
        "l1_wqdT": asc(np.asarray(inputs["L1_ffn_wq_down"], np.int32).T),
        "l1_sg": asc(np.asarray(inputs["L1_ffn_s_gate"], np.float32).reshape(ISC, 128).T),
        "l1_su": asc(np.asarray(inputs["L1_ffn_s_up"], np.float32).reshape(ISC, 128).T),
        "l1_sd": arr_fm(inputs["L1_ffn_s_down"]),
        "ones_col_bf": np.ones((128, 1), dtype=BF),
        "ones_row_f": np.ones((1, 128), dtype=np.float32),
        "ident_f": np.eye(128, dtype=np.float32),
    }

    half = DH // 2
    inv_freq = (10000.0 ** (-np.arange(half, dtype=np.float64) / half))
    in_maps = []
    for c in range(NCORES):
        off = (c % 4) * TC
        pos = off + np.arange(TC, dtype=np.float64)
        ang = pos[None, :] * inv_freq[:, None]
        cc = np.cos(ang)
        ss = np.sin(ang)
        ctab = np.concatenate([cc, cc], axis=0).astype(BF)
        stab = np.concatenate([-ss, ss], axis=0).astype(BF)
        m = dict(shared)
        m["x0T"] = asc(xflat[c * TC:(c + 1) * TC, :].T)
        m["ropec"] = asc(ctab)
        m["ropes"] = asc(stab)
        in_maps.append(m)
    return in_maps


def kernel(**inputs):
    in_maps = stage_inputs(inputs)
    nc, _ = build_nc(upto=4)
    res = run_bass_kernel_spmd(nc, in_maps, core_ids=list(range(NCORES)))
    out = np.empty((NCORES * TC, D), dtype=np.float32)
    for c in range(NCORES):
        out[c * TC:(c + 1) * TC, :] = np.asarray(res.results[c]["yT"]).T
    return out.reshape(2, S, D)
